# version 14
# baseline (speedup 1.0000x reference)
import numpy as np
import ml_dtypes
from contextlib import ExitStack

import concourse.bass as bass
import concourse.mybir as mybir
from concourse.bass_utils import run_bass_kernel_spmd

F32 = mybir.dt.float32
BF16 = mybir.dt.bfloat16
I32 = mybir.dt.int32
AF = mybir.ActivationFunctionType
ALU = mybir.AluOpType
AX = mybir.AxisListType

NCORES = 8
T = 1024
D = 4096
NI = 11008
NIT = 86
EPS = 1e-5


def _p4_seq():
    seq, ra, sa = [], 0, 48
    for j in range(NIT):
        if (sa >= NIT) or (ra < 48 and (j * 48) // NIT >= ra):
            seq.append(ra); ra += 1
        else:
            seq.append(sa); sa += 1
    return seq


P4SEQ = _p4_seq()


def build_nc():
    nc = bass.Bass("TRN2", target_bir_lowering=False)

    d_x = nc.dram_tensor("x", [T, D], F32, kind="ExternalInput").ap()
    d_xT = nc.dram_tensor("xT", [D, T], F32, kind="ExternalInput").ap()
    d_hwt = nc.dram_tensor("hwt", [32, 128, 32, 128], F32, kind="ExternalInput").ap()
    d_mask = nc.dram_tensor("mask", [D, D], I32, kind="ExternalInput").ap()
    d_hwg = nc.dram_tensor("hwg", [NIT, 128, 32, 128], I32, kind="ExternalInput").ap()
    d_hwu = nc.dram_tensor("hwu", [NIT, 128, 32, 128], I32, kind="ExternalInput").ap()
    d_hwd = nc.dram_tensor("hwd", [NIT, 128, 8, 4, 128], I32, kind="ExternalInput").ap()
    d_sgT = nc.dram_tensor("sgT", [128, NIT], F32, kind="ExternalInput").ap()
    d_suT = nc.dram_tensor("suT", [128, NIT], F32, kind="ExternalInput").ap()
    d_sdT = nc.dram_tensor("sdT", [128, 32], F32, kind="ExternalInput").ap()
    d_e32 = nc.dram_tensor("e32f", [128, D], BF16, kind="ExternalInput").ap()
    d_gm4 = nc.dram_tensor("gmat4", [128, 4], F32, kind="ExternalInput").ap()
    d_idb = nc.dram_tensor("identb", [128, 128], BF16, kind="ExternalInput").ap()
    d_out = nc.dram_tensor("outT", [D, T], F32, kind="ExternalOutput").ap()
    d_hTd = nc.dram_tensor("hTd", [NIT, 128, T], BF16, kind="Internal").ap()
    d_sTd = nc.dram_tensor("sTd", [D, T], BF16, kind="Internal").ap()

    with ExitStack() as top:
        sem = lambda n: top.enter_context(nc.semaphore(n))
        sem2 = lambda n: [sem(n + "0"), sem(n + "1")]
        sem4 = lambda n: [sem(n + str(i)) for i in range(4)]
        sm_hT_st = sem2("hT_st")
        sm_sT_st = sem2("sT_st")
        sm_cst = [sem(f"cstp{i}") for i in range(6)]
        sm_mask = sem2("mask")
        sm_x = sem4("xld")
        p1v, p1s, p1t = sem("p1v"), sem("p1s"), sem("p1t")
        sm_wt = sem2("wt")
        sm_cst2 = sem("cst2")
        p2v, p2t = sem("p2v"), sem("p2t")
        sm_wg, sm_wu = sem2("wg"), sem2("wu")
        p3v, p3s, p3t = sem("p3v"), sem("p3s"), sem("p3t")
        sm_hT_ld, sm_wd = sem4("hT_ld"), sem4("wd")
        sm_hr = [sem(f"hr{i}") for i in range(6)]
        sm_xT, sm_sT_ld, sm_out = sem4("xT"), sem4("sT_ld"), sem4("out")
        p4v, p4s, p4t = sem("p4v"), sem("p4s"), sem("p4t")

        sb = lambda n, s, d: top.enter_context(nc.sbuf_tensor(n, s, d))
        sg_cols = sb("sg_cols", [128, NIT], F32)
        su_cols = sb("su_cols", [128, NIT], F32)
        sd_cols = sb("sd_cols", [128, 32], F32)
        wdb = sb("wdb", [128, 16 * 512], BF16)

        with ExitStack() as s123:
            sb3 = lambda n, s, d: s123.enter_context(nc.sbuf_tensor(n, s, d))
            xnT = sb3("xnT", [128, 32 * T], BF16)

            if True:
                with ExitStack() as s12:
                    sb2l = lambda n, s, d: s12.enter_context(nc.sbuf_tensor(n, s, d))
                    ps2 = lambda n, s, d: s12.enter_context(nc.psum_tensor(n, s, d))
                    xs = [sb2l(f"xs{i}", [128, D], F32) for i in range(2)]
                    xn1 = sb2l("xn1", [128, D], BF16)
                    idb_sb = sb2l("idb_sb", [128, 128], BF16)
                    e32_sb = sb2l("e32_sb", [128, D], BF16)
                    gm4_sb = sb2l("gm4_sb", [128, 4], F32)
                    st_sum = sb2l("st_sum", [128, 8], F32)
                    st_ssq = sb2l("st_ssq", [128, 8], F32)
                    st_mean = sb2l("st_mean", [128, 8], F32)
                    st_msq = sb2l("st_msq", [128, 8], F32)
                    st_m2 = sb2l("st_m2", [128, 8], F32)
                    st_var = sb2l("st_var", [128, 8], F32)
                    st_sqv = sb2l("st_sqv", [128, 8], F32)
                    st_rst = sb2l("st_rst", [128, 8], F32)
                    st_nmr = sb2l("st_nmr", [128, 8], F32)
                    eps_col = sb2l("eps_col", [128, 1], F32)
                    invd_col = sb2l("invd_col", [128, 1], F32)
                    ninvd_col = sb2l("ninvd_col", [128, 1], F32)
                    mk = [sb2l(f"mk{i}", [128, D // 4], I32) for i in range(2)]
                    red = [sb2l(f"red{i}", [128, 128], F32) for i in range(2)]
                    bm01t = [sb2l(f"bm01t{i}", [4, 128], BF16) for i in range(2)]
                    bmfeT = sb2l("bmfeT", [128, 128], BF16)
                    wts = [sb2l(f"wts{i}", [128, D], F32) for i in range(2)]
                    wtm = [sb2l(f"wtm{i}", [128, D], BF16) for i in range(2)]
                    sTs = [sb2l(f"sTs{i}", [128, T], BF16) for i in range(2)]
                    p_tr = ps2("p_tr", [128, 1024], BF16)
                    p_me4 = ps2("p_me4", [128, 512], F32)
                    p_sT = [[ps2(f"p_sT{i}_{h}", [128, 512], F32) for h in range(2)]
                            for i in range(2)]
                    p_bm = ps2("p_bm", [128, 512], F32)
                    p_fe = ps2("p_fe", [128, 1024], BF16)

                    NK = 52
                    TL, VL = {}, {}
                    tc = vc = 0
                    for k in range(NK):
                        if 0 <= k - 1 <= 31:
                            tc += 1; TL[("bm4", k - 1)] = tc
                        if 0 <= k - 2 <= 31:
                            tc += 1; TL[("tr4", k - 2)] = tc
                        if k >= 16 and (k - 16) % 4 == 0 and (k - 16) // 4 <= 7:
                            tc += 32; TL[("exp", (k - 16) // 4)] = tc
                        if k >= 2 and (k - 2) % 2 == 0 and (k - 2) // 2 <= 7:
                            j = (k - 2) // 2
                            tc += 1; TL[("trb", j, 0)] = tc
                            tc += 1; TL[("trb", j, 1)] = tc
                        if k >= 3 and (k - 3) % 2 == 0 and (k - 3) // 2 <= 7:
                            j = (k - 3) // 2
                            tc += 1; TL[("trb", j, 2)] = tc
                            tc += 1; TL[("trb", j, 3)] = tc
                        if 0 <= k - 18 <= 31:
                            tc += 1; TL[("main", k - 18)] = tc
                    vc = 3
                    for k in range(NK):
                        if k <= 31:
                            for q in range(4):
                                vc += 1; VL[("redq", 4 * k + q)] = vc
                            VL[("red", k)] = vc
                        if 0 <= k - 1 <= 31:
                            vc += 1; VL[("thresh", k - 1)] = vc
                        if 0 <= k - 2 <= 31:
                            vc += 1; VL[("feev", k - 2)] = vc
                        if k >= 2 and (k - 2) % 2 == 0 and (k - 2) // 2 <= 7:
                            j = (k - 2) // 2
                            vc += 1; VL[("trev", j, 0)] = vc
                            vc += 1; VL[("trev", j, 1)] = vc
                        if k >= 3 and (k - 3) % 2 == 0 and (k - 3) // 2 <= 7:
                            j = (k - 3) // 2
                            vc += 1; VL[("trev", j, 2)] = vc
                            vc += 1; VL[("trev", j, 3)] = vc
                        if k <= 15:
                            if k % 2 == 0:
                                vc += 1; VL[("suma", k // 2)] = vc
                            else:
                                vc += 7; VL[("nmr", (k - 1) // 2)] = vc
                                VL[("vare", (k - 1) // 2)] = vc - 2
                        if 0 <= k - 17 <= 31:
                            vc += 1; VL[("mult", k - 17)] = vc
                        if 0 <= k - 19 <= 31:
                            vc += 2; VL[("evac", k - 19)] = vc
                    s_sq = lambda j: 3 * j + 1
                    s_sqrt = lambda j: 3 * j + 2
                    s_xn = lambda j: 3 * j + 3

                    with nc.Block("p12") as blk:
                        @blk.sync
                        def _(sy):
                            sy.dma_start(xs[0][:, :], d_x[0:128, :]
                                         ).then_inc(sm_x[0], 16)
                            for ci, (dst, srcd) in enumerate(
                                    [(idb_sb[:, :], d_idb), (e32_sb[:, :], d_e32),
                                     (gm4_sb[:, :], d_gm4),
                                     (sg_cols[:, :], d_sgT),
                                     (su_cols[:, :], d_suT),
                                     (sd_cols[:, :], d_sdT)]):
                                sy.dma_start(dst, srcd).then_inc(sm_cst[ci], 16)
                            def wt_issue(k):
                                if k >= 2:
                                    sy.wait_ge(p2v, VL[("mult", k - 2)])
                                sy.dma_start(
                                    wts[k % 2][:, :].rearrange(
                                        "p (kb n) -> p kb n", n=128),
                                    d_hwt[k]).then_inc(sm_wt[k % 2], 16)

                            def x_issue(i):
                                sy.wait_ge(p1s, s_xn(i - 2))
                                sy.dma_start(xs[0][:, :],
                                             d_x[i * 128:(i + 1) * 128, :]
                                             ).then_inc(sm_x[0], 16)
                            wt_issue(0)
                            wt_issue(1)
                            x_issue(2)
                            x_issue(4)
                            x_issue(6)
                            for k in range(2, 32):
                                wt_issue(k)

                        @blk.scalar
                        def _(sc):
                            sc.dma_start(xs[1][:, :], d_x[128:256, :]
                                         ).then_inc(sm_x[1], 16)
                            for j in range(8):
                                sc.wait_ge(sm_x[j % 2], 16 * (j // 2 + 1))
                                if j >= 1:
                                    sc.wait_ge(p1s, s_xn(j - 1))
                                    sc.wait_ge(p2t, TL[("trb", j - 1, 3)])
                                sc.activation(xn1[:, :], xs[j % 2][:, :],
                                              AF.Square,
                                              accum_out=st_ssq[:, j:j + 1]
                                              ).then_inc(p1s, 1)
                                sc.wait_ge(p2v, VL[("vare", j)])
                                sc.activation(st_var[:, j:j + 1],
                                              st_sqv[:, j:j + 1],
                                              AF.Sqrt).then_inc(p1s, 1)
                                sc.wait_ge(p2v, VL[("nmr", j)])
                                sc.activation(xn1[:, :], xs[j % 2][:, :],
                                              AF.Identity, bias=st_nmr[:, j:j + 1],
                                              scale=st_rst[:, j:j + 1]
                                              ).then_inc(p1s, 1)
                                if j + 2 <= 7 and (j + 2) % 2 == 1:
                                    sc.wait_ge(p1s, s_xn(j))
                                    sc.dma_start(
                                        xs[1][:, :],
                                        d_x[(j + 2) * 128:(j + 3) * 128, :]
                                        ).then_inc(sm_x[1], 16)
                            for nt in range(32):
                                sc.wait_ge(p2v, VL[("evac", nt)])
                                sc.dma_start(d_sTd[nt * 128:(nt + 1) * 128, :],
                                             sTs[nt % 2][:, :]
                                             ).then_inc(sm_sT_st[nt % 2], 16)

                        @blk.gpsimd
                        def _(gp):
                            for k in range(32):
                                for q in range(4):
                                    gq = 4 * k + q
                                    if gq >= 2:
                                        gp.wait_ge(p2v, VL[("redq", gq - 2)])
                                    gp.dma_start(
                                        mk[gq % 2][:, :],
                                        d_mask[k * 128:(k + 1) * 128,
                                               q * 1024:(q + 1) * 1024]
                                        ).then_inc(sm_mask[gq % 2], 16)

                        @blk.tensor
                        def _(te):
                            tw = {}

                            def wge(sem_, val):
                                if tw.get(id(sem_), -1) < val:
                                    te.wait_ge(sem_, val)
                                    tw[id(sem_)] = val

                            def bm4(k):
                                wge(p2v, VL[("red", k)])
                                if k >= 1:
                                    wge(p2v, VL[("thresh", k - 1)])
                                te.matmul(p_bm[0:4, 0:128], gm4_sb[:, :],
                                          red[k % 2][:, :], start=True,
                                          stop=True).then_inc(p2t, 1)

                            def tr4(k):
                                wge(p2v, VL[("thresh", k)])
                                if k >= 1:
                                    wge(p2v, VL[("feev", k - 1)])
                                te.transpose(p_fe[:, 0:4], bm01t[k % 2][:, :],
                                             idb_sb[0:4, 0:4]).then_inc(p2t, 1)

                            def exp(G):
                                wge(p2v, VL[("feev", 4 * G + 3)])
                                if G >= 1:
                                    wge(p2v, VL[("mult", 4 * G - 1)])
                                for kt in range(32):
                                    ins = te.matmul(
                                        p_me4[:, kt * 16:kt * 16 + 16],
                                        e32_sb[:, kt * 128:(kt + 1) * 128],
                                        bmfeT[:, G * 16:(G + 1) * 16],
                                        start=True, stop=True)
                                ins.then_inc(p2t, 32)

                            def trb2(j, b0):
                                wge(p1s, s_xn(j))
                                for b in (b0, b0 + 1):
                                    gb = 4 * j + b
                                    if gb >= 1:
                                        pj, pb = (gb - 1) // 4, (gb - 1) % 4
                                        wge(p2v, VL[("trev", pj, pb)])
                                    for c in range(8):
                                        kc = b * 8 + c
                                        ins = te.transpose(
                                            p_tr[:, c * 128:(c + 1) * 128],
                                            xn1[:, kc * 128:(kc + 1) * 128],
                                            idb_sb[:, :])
                                        if c == 7:
                                            ins.then_inc(p2t, 1)

                            def main(nt):
                                par = nt % 2
                                wge(p2v, VL[("mult", nt)])
                                if nt >= 2:
                                    wge(p2v, VL[("evac", nt - 2)])
                                wge(p2v, VL[("trev", 7, 3)])
                                for kb in range(32):
                                    A = wtm[par][:, kb * 128:(kb + 1) * 128]
                                    for th in range(2):
                                        ins = te.matmul(
                                            p_sT[par][th][:, :], A,
                                            xnT[:, kb * T + th * 512:
                                                kb * T + th * 512 + 512],
                                            start=(kb == 0), stop=(kb == 31))
                                        if kb == 31 and th == 1:
                                            ins.then_inc(p2t, 1)
                            for i in range(6):
                                te.wait_ge(sm_cst[i], 16)
                            for k in range(NK):
                                if 0 <= k - 1 <= 31:
                                    bm4(k - 1)
                                if 0 <= k - 2 <= 31:
                                    tr4(k - 2)
                                if k >= 16 and (k - 16) % 4 == 0 and (k - 16) // 4 <= 7:
                                    exp((k - 16) // 4)
                                if k >= 2 and (k - 2) % 2 == 0 and (k - 2) // 2 <= 7:
                                    trb2((k - 2) // 2, 0)
                                if k >= 3 and (k - 3) % 2 == 0 and (k - 3) // 2 <= 7:
                                    trb2((k - 3) // 2, 2)
                                if 0 <= k - 18 <= 31:
                                    main(k - 18)

                        @blk.vector
                        def _(ve):
                            vcc = [0]

                            def vinc(ins):
                                ins.then_inc(p2v, 1)
                                vcc[0] += 1
                                return vcc[0]

                            def redf(k):
                                if k >= 2:
                                    ve.wait_ge(p2t, TL[("bm4", k - 2)])
                                for q in range(4):
                                    gq = 4 * k + q
                                    ve.wait_ge(sm_mask[gq % 2],
                                               16 * (gq // 2 + 1))
                                    vinc(ve.tensor_reduce(
                                        red[k % 2][:, q * 32:(q + 1) * 32],
                                        mk[gq % 2][:, :].rearrange(
                                            "p (s e) -> p s e", e=32),
                                        AX.X, ALU.add))

                            def thresh(k):
                                ve.wait_ge(p2t, TL[("bm4", k)])
                                if k >= 2:
                                    ve.wait_ge(p2t, TL[("tr4", k - 2)])
                                vinc(ve.tensor_scalar(bm01t[k % 2][:, :],
                                                      p_bm[0:4, 0:128],
                                                      0.0, None, ALU.is_gt))

                            def feev(k):
                                ve.wait_ge(p2t, TL[("tr4", k)])
                                vinc(ve.tensor_copy(bmfeT[:, 4 * k:4 * k + 4],
                                                    p_fe[:, 0:4]))

                            def ln_a(j):
                                ve.wait_ge(sm_x[j % 2], 16 * (j // 2 + 1))
                                vinc(ve.reduce_sum(st_sum[:, j:j + 1],
                                                   xs[j % 2][:, :], AX.X))

                            def ln_b(j):
                                sl = slice(j, j + 1)
                                ve.wait_ge(p1s, s_sq(j))
                                ve.wait_ge(p2v, VL[("suma", j)])
                                vinc(ve.tensor_scalar(st_mean[:, sl], st_sum[:, sl],
                                                      ninvd_col[:, 0:1], None,
                                                      ALU.mult))
                                vinc(ve.tensor_scalar(st_msq[:, sl], st_ssq[:, sl],
                                                      invd_col[:, 0:1], None,
                                                      ALU.mult))
                                ve.wait_ge(p2v, vcc[0] - 1)
                                vinc(ve.tensor_tensor(st_m2[:, sl], st_mean[:, sl],
                                                      st_mean[:, sl], ALU.mult))
                                ve.wait_ge(p2v, vcc[0])
                                vinc(ve.tensor_tensor(st_var[:, sl], st_msq[:, sl],
                                                      st_m2[:, sl], ALU.subtract))
                                ve.wait_ge(p2v, vcc[0])
                                vinc(ve.tensor_scalar(st_sqv[:, sl],
                                                      st_var[:, sl],
                                                      eps_col[:, 0:1], None,
                                                      ALU.add))
                                ve.wait_ge(p1s, s_sqrt(j))
                                vinc(ve.reciprocal(st_rst[:, sl], st_var[:, sl]))
                                ve.wait_ge(p2v, vcc[0])
                                vinc(ve.tensor_tensor(st_nmr[:, sl], st_mean[:, sl],
                                                      st_rst[:, sl], ALU.mult))

                            def trev2(j, b0):
                                for b in (b0, b0 + 1):
                                    ve.wait_ge(p2t, TL[("trb", j, b)])
                                    dst = xnT[:, :].rearrange(
                                        "p (kc t) -> p kc t", t=T)[
                                        :, b * 8:(b + 1) * 8,
                                        j * 128:(j + 1) * 128]
                                    srcp = p_tr[:, :].rearrange(
                                        "p (c t) -> p c t", t=128)
                                    vinc(ve.tensor_copy(dst, srcp))

                            def mult(nt):
                                par = nt % 2
                                ve.wait_ge(sm_wt[nt % 2], 16 * (nt // 2 + 1))
                                ve.wait_ge(p2t, TL[("exp", nt // 4)])
                                if nt >= 2:
                                    ve.wait_ge(p2t, TL[("main", nt - 2)])
                                ap1 = bass.AP(p_me4.ap().tensor, (nt % 4) * 4,
                                              [[512, 128], [16, 32], [1, 4], [0, 32]])
                                vinc(ve.tensor_tensor(
                                    wtm[par][:, :].rearrange(
                                        "p (a b c) -> p a b c", a=32, b=4),
                                    wts[par][:, :].rearrange(
                                        "p (a b c) -> p a b c", a=32, b=4),
                                    ap1, ALU.mult))

                            def evac(nt):
                                par = nt % 2
                                ve.wait_ge(p2t, TL[("main", nt)])
                                if nt >= 2:
                                    ve.wait_ge(sm_sT_st[nt % 2], 16 * (nt // 2))
                                for th in range(2):
                                    vinc(ve.tensor_copy(
                                        sTs[par][:, th * 512:th * 512 + 512],
                                        p_sT[par][th][:, :]))
                            vinc(ve.memset(eps_col[:, :], EPS))
                            vinc(ve.memset(invd_col[:, :], 1.0 / D))
                            vinc(ve.memset(ninvd_col[:, :], -1.0 / D))
                            for k in range(NK):
                                if k <= 31:
                                    redf(k)
                                if 0 <= k - 1 <= 31:
                                    thresh(k - 1)
                                if 0 <= k - 2 <= 31:
                                    feev(k - 2)
                                if k >= 2 and (k - 2) % 2 == 0 and (k - 2) // 2 <= 7:
                                    trev2((k - 2) // 2, 0)
                                if k >= 3 and (k - 3) % 2 == 0 and (k - 3) // 2 <= 7:
                                    trev2((k - 3) // 2, 2)
                                if k <= 15:
                                    if k % 2 == 0:
                                        ln_a(k // 2)
                                    else:
                                        ln_b((k - 1) // 2)
                                if 0 <= k - 17 <= 31:
                                    mult(k - 17)
                                if 0 <= k - 19 <= 31:
                                    evac(k - 19)
                            assert vcc[0] == vc, (vcc[0], vc)

            with ExitStack() as s3:
                sb3l = lambda n, s, d: s3.enter_context(nc.sbuf_tensor(n, s, d))
                ps3 = lambda n, s, d: s3.enter_context(nc.psum_tensor(n, s, d))
                wgb = [sb3l(f"wgb{i}", [128, D], BF16) for i in range(2)]
                wub = [sb3l(f"wub{i}", [128, D], BF16) for i in range(2)]
                sil = [sb3l(f"sil{i}", [128, T], F32) for i in range(2)]
                uss = [sb3l(f"uss{i}", [128, T], F32) for i in range(2)]
                hTt = [sb3l(f"hTt{i}", [128, T], BF16) for i in range(2)]
                p_g = [[ps3(f"p_g{i}_{h}", [128, 512], F32) for h in range(2)]
                       for i in range(2)]
                p_u = [[ps3(f"p_u{i}_{h}", [128, 512], F32) for h in range(2)]
                       for i in range(2)]

                with nc.Block("ffnA") as blk:
                    @blk.gpsimd
                    def _(gp):
                        for it in range(NIT):
                            if it >= 2:
                                gp.wait_ge(p3t, it - 1)
                            gp.dma_start(
                                wgb[it % 2][:, :].rearrange("p (b i) -> p b i", i=128),
                                d_hwg[it]).then_inc(sm_wg[it % 2], 16)
                            gp.dma_start(
                                wub[it % 2][:, :].rearrange("p (b i) -> p b i", i=128),
                                d_hwu[it]).then_inc(sm_wu[it % 2], 16)
                        for gb in range(2):
                            gp.dma_start(
                                wdb[:, gb * 2048:gb * 2048 + 2048].rearrange(
                                    "p (i hh) -> p i hh", i=4),
                                d_hwd[gb * 4:gb * 4 + 4, :, 0, :, :].rearrange(
                                    "i p a b -> p i (a b)")
                                ).then_inc(sm_wd[gb], 16)

                    @blk.tensor
                    def _(te):
                        for it in range(NIT):
                            par = it % 2
                            te.wait_ge(sm_wg[it % 2], 16 * (it // 2 + 1))
                            te.wait_ge(sm_wu[it % 2], 16 * (it // 2 + 1))
                            if it >= 2:
                                te.wait_ge(p3s, 2 * (it - 2) + 2)
                                te.wait_ge(p3v, 3 * (it - 2) + 2)
                            for w_b, p_x in [(wgb[par], p_g[par]), (wub[par], p_u[par])]:
                                for blk_h in range(32):
                                    A = w_b[:, blk_h * 128:(blk_h + 1) * 128]
                                    for th in range(2):
                                        ins = te.matmul(
                                            p_x[th][:, :], A,
                                            xnT[:, blk_h * T + th * 512:
                                                blk_h * T + th * 512 + 512],
                                            start=(blk_h == 0), stop=(blk_h == 31))
                            ins.then_inc(p3t, 1)

                    @blk.scalar
                    def _(sc):
                        for it in range(NIT):
                            par = it % 2
                            if it >= 1:
                                sc.wait_ge(p3v, 3 * (it - 1) + 3)
                                sc.dma_start(d_hTd[it - 1], hTt[(it - 1) % 2][:, :]
                                             ).then_inc(sm_hT_st[(it - 1) % 2], 16)
                            sc.wait_ge(p3t, it + 1)
                            if it >= 2:
                                sc.wait_ge(p3v, 3 * (it - 2) + 3)
                            for th in range(2):
                                sc.activation(sil[par][:, th * 512:th * 512 + 512],
                                              p_g[par][th][:, :], AF.Silu,
                                              scale=sg_cols[:, it:it + 1]
                                              ).then_inc(p3s, 1)
                        sc.wait_ge(p3v, 3 * (NIT - 1) + 3)
                        sc.dma_start(d_hTd[NIT - 1], hTt[(NIT - 1) % 2][:, :]
                                     ).then_inc(sm_hT_st[(NIT - 1) % 2], 16)

                    @blk.vector
                    def _(ve):
                        for it in range(NIT):
                            par = it % 2
                            ve.wait_ge(p3t, it + 1)
                            if it >= 2:
                                ve.wait_ge(p3v, 3 * it - 3)
                            for th in range(2):
                                ve.tensor_scalar(uss[par][:, th * 512:th * 512 + 512],
                                                 p_u[par][th][:, :],
                                                 su_cols[:, it:it + 1], None,
                                                 ALU.mult).then_inc(p3v, 1)
                            ve.wait_ge(p3s, 2 * it + 2)
                            if it >= 2:
                                ve.wait_ge(sm_hT_st[it % 2], 16 * (it // 2))
                            ve.wait_ge(p3v, 3 * it + 2)
                            ve.tensor_tensor(hTt[par][:, :], sil[par][:, :],
                                             uss[par][:, :], ALU.mult).then_inc(p3v, 1)

        with ExitStack() as s4:
            sb4 = lambda n, s, d: s4.enter_context(nc.sbuf_tensor(n, s, d))
            ps4 = lambda n, s, d: s4.enter_context(nc.psum_tensor(n, s, d))
            hTr = sb4("hTr", [128, 48 * 1024], BF16)
            hTl = [sb4(f"hTl{i}", [128, 2048], BF16) for i in range(4)]
            gTs = [sb4(f"gTs{i}", [128, T], F32) for i in range(4)]
            sTl = [sb4(f"sTl{i}", [128, T], F32) for i in range(4)]
            xTl = [sb4(f"xTl{i}", [128, T], F32) for i in range(4)]
            outs = [sb4(f"outs{i}", [128, T], F32) for i in range(4)]
            tmpm = sb4("tmpm", [128, T], F32)
            p_acc = [[ps4(f"p_acc{h}_{t}", [128, 512], F32) for t in range(2)]
                     for h in range(4)]

            NB = 22
            NPAIR = 19
            SEQ = P4SEQ
            assert sorted(SEQ) == list(range(NIT))
            POS = {it: j for j, it in enumerate(SEQ)}
            ht_use = {}
            _cnt4 = [0, 0, 0, 0]
            for _g in range(8):
                for _p in range(NPAIR):
                    _cnt4[_p % 4] += 1
                    ht_use[(_g, _p)] = _cnt4[_p % 4]
            wd_use = {}
            _cnt4w = [0, 0, 0, 0]
            for _g in range(8):
                for _b in range(NB):
                    _gb = _g * NB + _b
                    _cnt4w[_gb % 4] += 1
                    wd_use[(_g, _b)] = _cnt4w[_gb % 4]

            with nc.Block("down") as blk:
                @blk.sync
                def _(sy):
                    for r in range(6):
                        sy.wait_ge(sm_hT_st[0], 16 * (4 * r + 4))
                        sy.wait_ge(sm_hT_st[1], 16 * (4 * r + 4))
                        sy.dma_start(
                            hTr[:, r * 8192:(r + 1) * 8192].rearrange(
                                "q (i t) -> q i t", i=8),
                            d_hTd[8 * r:8 * r + 8, :, :].rearrange(
                                "i q t -> q i t")).then_inc(sm_hr[r], 16)
                    sy.wait_ge(sm_hT_st[0], 16 * 43)
                    sy.wait_ge(sm_hT_st[1], 16 * 43)
                    sy.wait_ge(sm_sT_st[0], 16 * 16)
                    sy.wait_ge(sm_sT_st[1], 16 * 16)
                    for g in range(8):
                        for p in range(NPAIR):
                            if p >= 4:
                                sy.wait_ge(p4t,
                                           g * NIT + POS[48 + 2 * (p - 4) + 1] + 1)
                            elif g >= 1:
                                q = {0: 16, 1: 17, 2: 18, 3: 15}[p]
                                sy.wait_ge(p4t,
                                           (g - 1) * NIT + POS[48 + 2 * q + 1] + 1)
                            sy.dma_start(
                                hTl[p % 4][:, :].rearrange(
                                    "q (i t) -> q i t", i=2),
                                d_hTd[48 + 2 * p:48 + 2 * p + 2, :, :].rearrange(
                                    "i q t -> q i t")).then_inc(sm_hT_ld[p % 4], 16)
                            if p == 2:
                                for ht in range(4):
                                    hg = g * 4 + ht
                                    if g >= 1:
                                        sy.wait_ge(p4v, 8 * (g - 1) + 2 * (ht + 1))
                                    sy.dma_start(xTl[ht][:, :],
                                                 d_xT[hg * 128:(hg + 1) * 128, :]
                                                 ).then_inc(sm_xT[ht], 16)
                    for ht in range(4):
                        sy.wait_ge(sm_out[ht], 16 * 8)

                @blk.gpsimd
                def _(gp):
                    for g in range(8):
                        for b in range(NB):
                            it0 = b * 4
                            nit = min(4, NIT - it0)
                            gb = g * NB + b
                            if gb < 2:
                                continue
                            if gb >= 4:
                                pgb = gb - 4
                                pg, pb = pgb // NB, pgb % NB
                                gp.wait_ge(p4t,
                                           pg * NIT + min(4 * (pb + 1), NIT))
                            sl = (gb % 4) * 2048
                            dst = wdb[:, sl:sl + nit * 512]
                            gp.dma_start(
                                dst.rearrange("p (i hh) -> p i hh", i=nit),
                                d_hwd[it0:it0 + nit, :, g, :, :].rearrange(
                                    "i p a b -> p i (a b)")
                                ).then_inc(sm_wd[gb % 4], 16)
                        for ht in range(4):
                            hg = g * 4 + ht
                            if g >= 1:
                                gp.wait_ge(p4v, 8 * (g - 1) + 2 * (ht + 1))
                            gp.wait_ge(sm_sT_st[hg % 2], 16 * (hg // 2 + 1))
                            gp.dma_start(sTl[ht][:, :],
                                         d_sTd[hg * 128:(hg + 1) * 128, :]
                                         ).then_inc(sm_sT_ld[ht], 16)

                @blk.tensor
                def _(te):
                    for g in range(8):
                        for j in range(NIT):
                            it = SEQ[j]
                            b = j // 4
                            gb = g * NB + b
                            te.wait_ge(sm_wd[gb % 4], 16 * wd_use[(g, b)])
                            if it < 48:
                                if g == 0:
                                    te.wait_ge(sm_hr[it // 8], 16)
                                slot, sbase = hTr, it * 1024
                            else:
                                pr = (it - 48) // 2
                                te.wait_ge(sm_hT_ld[pr % 4],
                                           16 * ht_use[(g, pr)])
                                slot, sbase = hTl[pr % 4], (it % 2) * 1024
                            base = (gb % 4) * 2048 + (j % 4) * 512
                            for ht in range(4):
                                A = wdb[:, base + ht * 128:base + ht * 128 + 128]
                                for th in range(2):
                                    if j == 0 and g >= 1:
                                        te.wait_ge(p4s,
                                                   8 * (g - 1) + 2 * ht + th + 1)
                                    B = slot[:, sbase + th * 512:
                                             sbase + th * 512 + 512]
                                    ins = te.matmul(p_acc[ht][th][:, :], A, B,
                                                    start=(j == 0), stop=(j == NIT - 1))
                            ins.then_inc(p4t, 1)

                @blk.scalar
                def _(sc):
                    for g in range(8):
                        sc.wait_ge(p4t, g * NIT + NIT)
                        for ht in range(4):
                            if g >= 1:
                                sc.wait_ge(p4v, 8 * (g - 1) + 2 * (ht + 1))
                            for th in range(2):
                                sc.activation(gTs[ht][:, th * 512:th * 512 + 512],
                                              p_acc[ht][th][:, :], AF.Silu,
                                              scale=sd_cols[:, g * 4 + ht:g * 4 + ht + 1]
                                              ).then_inc(p4s, 1)
                        for ht in range(4):
                            hg = g * 4 + ht
                            sc.wait_ge(p4v, 8 * g + 2 * (ht + 1))
                            sc.dma_start(d_out[hg * 128:(hg + 1) * 128, :],
                                         outs[ht][:, :]).then_inc(sm_out[ht], 16)

                @blk.vector
                def _(ve):
                    for g in range(8):
                        for ht in range(4):
                            k = 4 * g + ht
                            ve.wait_ge(p4s, 8 * g + 2 * (ht + 1))
                            ve.wait_ge(sm_xT[ht], 16 * (g + 1))
                            ve.wait_ge(sm_sT_ld[ht], 16 * (g + 1))
                            if g >= 1:
                                ve.wait_ge(sm_out[ht], 16 * g)
                            if k >= 1:
                                ve.wait_ge(p4v, 2 * k)
                            ve.tensor_tensor(tmpm[:, :], gTs[ht][:, :], sTl[ht][:, :],
                                             ALU.mult).then_inc(p4v, 1)
                            ve.wait_ge(p4v, 2 * k + 1)
                            ve.tensor_tensor(outs[ht][:, :], tmpm[:, :], xTl[ht][:, :],
                                             ALU.add).then_inc(p4v, 1)

    nc.finalize()
    return nc


_NC_CACHE = None


def kernel(L0_attn_data, L0_attn_weight, L0_attn_mask, L0_ffn_w_gate, L0_ffn_s_gate,
           L0_ffn_w_up, L0_ffn_s_up, L0_ffn_w_down, L0_ffn_s_down):
    global _NC_CACHE
    asc = np.ascontiguousarray

    x2 = np.asarray(L0_attn_data, dtype=np.float32).reshape(NCORES * T, D)
    w = np.asarray(L0_attn_weight, dtype=np.float32)
    mask = np.asarray(L0_attn_mask, dtype=np.int32)
    wg = np.asarray(L0_ffn_w_gate, dtype=np.int32)
    wu = np.asarray(L0_ffn_w_up, dtype=np.int32)
    wd = np.asarray(L0_ffn_w_down, dtype=np.int32)
    sg = np.asarray(L0_ffn_s_gate, dtype=np.float32)
    su = np.asarray(L0_ffn_s_up, dtype=np.float32)
    sd = np.asarray(L0_ffn_s_down, dtype=np.float32)

    hwt = asc(w.reshape(32, 128, 32, 128).transpose(0, 3, 2, 1))
    hwg = asc(wg.reshape(NIT, 128, 32, 128).transpose(0, 3, 2, 1))
    hwu = asc(wu.reshape(NIT, 128, 32, 128).transpose(0, 3, 2, 1))
    hwd = asc(wd.reshape(8, 4, 128, NIT, 128).transpose(3, 4, 0, 1, 2)[P4SEQ])
    sgT = asc(sg.reshape(NIT, 128).T)
    suT = asc(su.reshape(NIT, 128).T)
    sdT = asc(sd.reshape(32, 128).T)

    c = np.arange(D)
    p = np.arange(128)[:, None]
    e32f = (c // 32 == p).astype(ml_dtypes.bfloat16)
    gmat4 = (np.arange(4) == p // 32).astype(np.float32)
    identb = np.eye(128, dtype=ml_dtypes.bfloat16)

    shared = dict(hwt=hwt, mask=mask, hwg=hwg, hwu=hwu, hwd=hwd, sgT=sgT, suT=suT,
                  sdT=sdT, e32f=e32f, gmat4=gmat4, identb=identb)
    in_maps = []
    for ci in range(NCORES):
        xc = asc(x2[ci * T:(ci + 1) * T])
        in_maps.append(dict(x=xc, xT=asc(xc.T), **shared))

    if _NC_CACHE is None:
        _NC_CACHE = build_nc()
    res = run_bass_kernel_spmd(_NC_CACHE, in_maps, core_ids=list(range(NCORES)))
    out = np.concatenate([np.asarray(r["outT"]).T for r in res.results], axis=0)
    return asc(out.reshape(4, 2048, D).astype(np.float32))


# version 15
# speedup vs baseline: 1.0119x; 1.0119x over previous
import numpy as np
import ml_dtypes
from contextlib import ExitStack

import concourse.bass as bass
import concourse.mybir as mybir
from concourse.bass_utils import run_bass_kernel_spmd

F32 = mybir.dt.float32
BF16 = mybir.dt.bfloat16
I32 = mybir.dt.int32
AF = mybir.ActivationFunctionType
ALU = mybir.AluOpType
AX = mybir.AxisListType

NCORES = 8
T = 1024
D = 4096
NI = 11008
NIT = 86
EPS = 1e-5


def _p4_seq():
    seq, ra, sa = [], 0, 48
    for j in range(NIT):
        if (sa >= NIT) or (ra < 48 and (j * 48) // NIT >= ra):
            seq.append(ra); ra += 1
        else:
            seq.append(sa); sa += 1
    return seq


P4SEQ = _p4_seq()


def build_nc():
    nc = bass.Bass("TRN2", target_bir_lowering=False)

    d_x = nc.dram_tensor("x", [T, D], F32, kind="ExternalInput").ap()
    d_xT = nc.dram_tensor("xT", [D, T], F32, kind="ExternalInput").ap()
    d_hwt = nc.dram_tensor("hwt", [32, 128, 32, 128], F32, kind="ExternalInput").ap()
    d_mask = nc.dram_tensor("mask", [D, D], I32, kind="ExternalInput").ap()
    d_hwg = nc.dram_tensor("hwg", [NIT, 128, 32, 128], I32, kind="ExternalInput").ap()
    d_hwu = nc.dram_tensor("hwu", [NIT, 128, 32, 128], I32, kind="ExternalInput").ap()
    d_hwd = nc.dram_tensor("hwd", [NIT, 128, 8, 4, 128], I32, kind="ExternalInput").ap()
    d_sgT = nc.dram_tensor("sgT", [128, NIT], F32, kind="ExternalInput").ap()
    d_suT = nc.dram_tensor("suT", [128, NIT], F32, kind="ExternalInput").ap()
    d_sdT = nc.dram_tensor("sdT", [128, 32], F32, kind="ExternalInput").ap()
    d_e32 = nc.dram_tensor("e32f", [128, D], BF16, kind="ExternalInput").ap()
    d_gm4 = nc.dram_tensor("gmat4", [128, 4], F32, kind="ExternalInput").ap()
    d_idb = nc.dram_tensor("identb", [128, 128], BF16, kind="ExternalInput").ap()
    d_out = nc.dram_tensor("outT", [D, T], F32, kind="ExternalOutput").ap()
    d_hTd = nc.dram_tensor("hTd", [NIT, 128, T], BF16, kind="Internal").ap()
    d_sTd = nc.dram_tensor("sTd", [D, T], BF16, kind="Internal").ap()

    with ExitStack() as top:
        sem = lambda n: top.enter_context(nc.semaphore(n))
        sem2 = lambda n: [sem(n + "0"), sem(n + "1")]
        sem4 = lambda n: [sem(n + str(i)) for i in range(4)]
        sm_hT_st = sem2("hT_st")
        sm_sT_st = sem2("sT_st")
        sm_cst = [sem(f"cstp{i}") for i in range(6)]
        sm_mask = sem2("mask")
        sm_x = sem4("xld")
        p1v, p1s, p1t = sem("p1v"), sem("p1s"), sem("p1t")
        sm_wt = sem2("wt")
        sm_cst2 = sem("cst2")
        p2v, p2t = sem("p2v"), sem("p2t")
        sm_wg, sm_wu = sem2("wg"), sem2("wu")
        p3v, p3s, p3t = sem("p3v"), sem("p3s"), sem("p3t")
        sm_hT_ld, sm_wd = sem4("hT_ld"), sem4("wd")
        sm_hr = [sem(f"hr{i}") for i in range(6)]
        sm_xT, sm_sT_ld, sm_out = sem4("xT"), sem4("sT_ld"), sem4("out")
        p4v, p4s, p4t = sem("p4v"), sem("p4s"), sem("p4t")

        sb = lambda n, s, d: top.enter_context(nc.sbuf_tensor(n, s, d))
        sg_cols = sb("sg_cols", [128, NIT], F32)
        su_cols = sb("su_cols", [128, NIT], F32)
        sd_cols = sb("sd_cols", [128, 32], F32)
        wdb = sb("wdb", [128, 16 * 512], BF16)

        with ExitStack() as s123:
            sb3 = lambda n, s, d: s123.enter_context(nc.sbuf_tensor(n, s, d))
            xnT = sb3("xnT", [128, 32 * T], BF16)

            if True:
                with ExitStack() as s12:
                    sb2l = lambda n, s, d: s12.enter_context(nc.sbuf_tensor(n, s, d))
                    ps2 = lambda n, s, d: s12.enter_context(nc.psum_tensor(n, s, d))
                    xs = [sb2l(f"xs{i}", [128, D], F32) for i in range(2)]
                    xn = [sb2l(f"xn{i}", [128, D], BF16) for i in range(2)]
                    idb_sb = sb2l("idb_sb", [128, 128], BF16)
                    e32_sb = sb2l("e32_sb", [128, D], BF16)
                    gm4_sb = sb2l("gm4_sb", [128, 4], F32)
                    st_sum = sb2l("st_sum", [128, 8], F32)
                    st_ssq = sb2l("st_ssq", [128, 8], F32)
                    st_mean = sb2l("st_mean", [128, 8], F32)
                    st_msq = sb2l("st_msq", [128, 8], F32)
                    st_m2 = sb2l("st_m2", [128, 8], F32)
                    st_var = sb2l("st_var", [128, 8], F32)
                    st_sqv = sb2l("st_sqv", [128, 8], F32)
                    st_rst = sb2l("st_rst", [128, 8], F32)
                    st_nmr = sb2l("st_nmr", [128, 8], F32)
                    eps_col = sb2l("eps_col", [128, 1], F32)
                    invd_col = sb2l("invd_col", [128, 1], F32)
                    ninvd_col = sb2l("ninvd_col", [128, 1], F32)
                    mk = [sb2l(f"mk{i}", [128, D // 4], I32) for i in range(2)]
                    red = [sb2l(f"red{i}", [128, 128], F32) for i in range(2)]
                    bm01t = [sb2l(f"bm01t{i}", [4, 128], BF16) for i in range(2)]
                    bmfeT = sb2l("bmfeT", [128, 128], BF16)
                    wts = [sb2l(f"wts{i}", [128, D], F32) for i in range(2)]
                    wtm = [sb2l(f"wtm{i}", [128, D], BF16) for i in range(2)]
                    sTs = [sb2l(f"sTs{i}", [128, T], BF16) for i in range(2)]
                    p_tr = [ps2(f"p_tr{i}", [128, 1024], BF16) for i in range(2)]
                    p_me4 = ps2("p_me4", [128, 512], F32)
                    p_sT = [[ps2(f"p_sT{i}_{h}", [128, 512], F32) for h in range(2)]
                            for i in range(2)]
                    p_bm = ps2("p_bm", [128, 512], F32)

                    NK = 52
                    TL, VL = {}, {}
                    PTRSEQ = []
                    for k in range(NK):
                        if 0 <= k - 2 <= 31:
                            PTRSEQ.append(("tr4", k - 2))
                        if k >= 2 and (k - 2) % 2 == 0 and (k - 2) // 2 <= 7:
                            j = (k - 2) // 2
                            PTRSEQ.append(("trb", j, 0))
                            PTRSEQ.append(("trb", j, 1))
                        if k >= 3 and (k - 3) % 2 == 0 and (k - 3) // 2 <= 7:
                            j = (k - 3) // 2
                            PTRSEQ.append(("trb", j, 2))
                            PTRSEQ.append(("trb", j, 3))
                    PTRIX = {u: i for i, u in enumerate(PTRSEQ)}
                    tc = vc = 0
                    for k in range(NK):
                        if 0 <= k - 1 <= 31:
                            tc += 1; TL[("bm4", k - 1)] = tc
                        if 0 <= k - 2 <= 31:
                            tc += 1; TL[("tr4", k - 2)] = tc
                        if k >= 16 and (k - 16) % 4 == 0 and (k - 16) // 4 <= 7:
                            tc += 32; TL[("exp", (k - 16) // 4)] = tc
                        if k >= 2 and (k - 2) % 2 == 0 and (k - 2) // 2 <= 7:
                            j = (k - 2) // 2
                            tc += 1; TL[("trb", j, 0)] = tc
                            tc += 1; TL[("trb", j, 1)] = tc
                        if k >= 3 and (k - 3) % 2 == 0 and (k - 3) // 2 <= 7:
                            j = (k - 3) // 2
                            tc += 1; TL[("trb", j, 2)] = tc
                            tc += 1; TL[("trb", j, 3)] = tc
                        if 0 <= k - 18 <= 31:
                            tc += 1; TL[("main", k - 18)] = tc
                    vc = 3
                    for k in range(NK):
                        if k <= 31:
                            for q in range(4):
                                vc += 1; VL[("redq", 4 * k + q)] = vc
                            VL[("red", k)] = vc
                        if 0 <= k - 1 <= 31:
                            vc += 1; VL[("thresh", k - 1)] = vc
                        if 0 <= k - 2 <= 31:
                            vc += 1; VL[("feev", k - 2)] = vc
                            VL[("ptr_ev", PTRIX[("tr4", k - 2)])] = vc
                        if k >= 2 and (k - 2) % 2 == 0 and (k - 2) // 2 <= 7:
                            j = (k - 2) // 2
                            for b in (0, 1):
                                vc += 1; VL[("trev", j, b)] = vc
                                VL[("ptr_ev", PTRIX[("trb", j, b)])] = vc
                        if k >= 3 and (k - 3) % 2 == 0 and (k - 3) // 2 <= 7:
                            j = (k - 3) // 2
                            for b in (2, 3):
                                vc += 1; VL[("trev", j, b)] = vc
                                VL[("ptr_ev", PTRIX[("trb", j, b)])] = vc
                        if k <= 15:
                            if k % 2 == 0:
                                vc += 1; VL[("suma", k // 2)] = vc
                            else:
                                vc += 7; VL[("nmr", (k - 1) // 2)] = vc
                                VL[("vare", (k - 1) // 2)] = vc - 2
                        if 0 <= k - 17 <= 31:
                            vc += 1; VL[("mult", k - 17)] = vc
                        if 0 <= k - 19 <= 31:
                            vc += 2; VL[("evac", k - 19)] = vc
                    s_sq = lambda j: 3 * j + 1
                    s_sqrt = lambda j: 3 * j + 2
                    s_xn = lambda j: 3 * j + 3

                    with nc.Block("p12") as blk:
                        @blk.sync
                        def _(sy):
                            sy.dma_start(xs[0][:, :], d_x[0:128, :]
                                         ).then_inc(sm_x[0], 16)
                            for ci, (dst, srcd) in enumerate(
                                    [(idb_sb[:, :], d_idb), (e32_sb[:, :], d_e32),
                                     (gm4_sb[:, :], d_gm4),
                                     (sg_cols[:, :], d_sgT),
                                     (su_cols[:, :], d_suT),
                                     (sd_cols[:, :], d_sdT)]):
                                sy.dma_start(dst, srcd).then_inc(sm_cst[ci], 16)
                            def wt_issue(k):
                                if k >= 2:
                                    sy.wait_ge(p2v, VL[("mult", k - 2)])
                                sy.dma_start(
                                    wts[k % 2][:, :].rearrange(
                                        "p (kb n) -> p kb n", n=128),
                                    d_hwt[k]).then_inc(sm_wt[k % 2], 16)

                            def x_issue(i):
                                sy.wait_ge(p1s, s_xn(i - 2))
                                sy.dma_start(xs[0][:, :],
                                             d_x[i * 128:(i + 1) * 128, :]
                                             ).then_inc(sm_x[0], 16)
                            wt_issue(0)
                            wt_issue(1)
                            x_issue(2)
                            x_issue(4)
                            x_issue(6)
                            for k in range(2, 32):
                                wt_issue(k)

                        @blk.scalar
                        def _(sc):
                            sc.dma_start(xs[1][:, :], d_x[128:256, :]
                                         ).then_inc(sm_x[1], 16)
                            for j in range(8):
                                sc.wait_ge(sm_x[j % 2], 16 * (j // 2 + 1))
                                if j >= 2:
                                    sc.wait_ge(p1s, s_xn(j - 2))
                                    sc.wait_ge(p2t, TL[("trb", j - 2, 3)])
                                sc.activation(xn[j % 2][:, :], xs[j % 2][:, :],
                                              AF.Square,
                                              accum_out=st_ssq[:, j:j + 1]
                                              ).then_inc(p1s, 1)
                                sc.wait_ge(p2v, VL[("vare", j)])
                                sc.activation(st_var[:, j:j + 1],
                                              st_sqv[:, j:j + 1],
                                              AF.Sqrt).then_inc(p1s, 1)
                                sc.wait_ge(p2v, VL[("nmr", j)])
                                sc.activation(xn[j % 2][:, :], xs[j % 2][:, :],
                                              AF.Identity, bias=st_nmr[:, j:j + 1],
                                              scale=st_rst[:, j:j + 1]
                                              ).then_inc(p1s, 1)
                                if j + 2 <= 7 and (j + 2) % 2 == 1:
                                    sc.wait_ge(p1s, s_xn(j))
                                    sc.dma_start(
                                        xs[1][:, :],
                                        d_x[(j + 2) * 128:(j + 3) * 128, :]
                                        ).then_inc(sm_x[1], 16)
                            for nt in range(32):
                                sc.wait_ge(p2v, VL[("evac", nt)])
                                sc.dma_start(d_sTd[nt * 128:(nt + 1) * 128, :],
                                             sTs[nt % 2][:, :]
                                             ).then_inc(sm_sT_st[nt % 2], 16)

                        @blk.gpsimd
                        def _(gp):
                            for k in range(32):
                                for q in range(4):
                                    gq = 4 * k + q
                                    if gq >= 2:
                                        gp.wait_ge(p2v, VL[("redq", gq - 2)])
                                    gp.dma_start(
                                        mk[gq % 2][:, :],
                                        d_mask[k * 128:(k + 1) * 128,
                                               q * 1024:(q + 1) * 1024]
                                        ).then_inc(sm_mask[gq % 2], 16)

                        @blk.tensor
                        def _(te):
                            tw = {}

                            def wge(sem_, val):
                                if tw.get(id(sem_), -1) < val:
                                    te.wait_ge(sem_, val)
                                    tw[id(sem_)] = val

                            def bm4(k):
                                wge(p2v, VL[("red", k)])
                                if k >= 1:
                                    wge(p2v, VL[("thresh", k - 1)])
                                te.matmul(p_bm[0:4, 0:128], gm4_sb[:, :],
                                          red[k % 2][:, :], start=True,
                                          stop=True).then_inc(p2t, 1)

                            def tr4(k):
                                wge(p2v, VL[("thresh", k)])
                                ix = PTRIX[("tr4", k)]
                                if ix >= 2:
                                    wge(p2v, VL[("ptr_ev", ix - 2)])
                                te.transpose(p_tr[ix % 2][:, 0:4],
                                             bm01t[k % 2][:, :],
                                             idb_sb[0:4, 0:4]).then_inc(p2t, 1)

                            def exp(G):
                                wge(p2v, VL[("feev", 4 * G + 3)])
                                if G >= 1:
                                    wge(p2v, VL[("mult", 4 * G - 1)])
                                for kt in range(32):
                                    ins = te.matmul(
                                        p_me4[:, kt * 16:kt * 16 + 16],
                                        e32_sb[:, kt * 128:(kt + 1) * 128],
                                        bmfeT[:, G * 16:(G + 1) * 16],
                                        start=True, stop=True)
                                ins.then_inc(p2t, 32)

                            def trb2(j, b0):
                                wge(p1s, s_xn(j))
                                for b in (b0, b0 + 1):
                                    ix = PTRIX[("trb", j, b)]
                                    if ix >= 2:
                                        wge(p2v, VL[("ptr_ev", ix - 2)])
                                    for c in range(8):
                                        kc = b * 8 + c
                                        ins = te.transpose(
                                            p_tr[ix % 2][:, c * 128:(c + 1) * 128],
                                            xn[j % 2][:, kc * 128:(kc + 1) * 128],
                                            idb_sb[:, :])
                                        if c == 7:
                                            ins.then_inc(p2t, 1)

                            def main(nt):
                                par = nt % 2
                                wge(p2v, VL[("mult", nt)])
                                if nt >= 2:
                                    wge(p2v, VL[("evac", nt - 2)])
                                wge(p2v, VL[("trev", 7, 3)])
                                for kb in range(32):
                                    A = wtm[par][:, kb * 128:(kb + 1) * 128]
                                    for th in range(2):
                                        ins = te.matmul(
                                            p_sT[par][th][:, :], A,
                                            xnT[:, kb * T + th * 512:
                                                kb * T + th * 512 + 512],
                                            start=(kb == 0), stop=(kb == 31))
                                        if kb == 31 and th == 1:
                                            ins.then_inc(p2t, 1)
                            for i in range(6):
                                te.wait_ge(sm_cst[i], 16)
                            for k in range(NK):
                                if 0 <= k - 1 <= 31:
                                    bm4(k - 1)
                                if 0 <= k - 2 <= 31:
                                    tr4(k - 2)
                                if k >= 16 and (k - 16) % 4 == 0 and (k - 16) // 4 <= 7:
                                    exp((k - 16) // 4)
                                if k >= 2 and (k - 2) % 2 == 0 and (k - 2) // 2 <= 7:
                                    trb2((k - 2) // 2, 0)
                                if k >= 3 and (k - 3) % 2 == 0 and (k - 3) // 2 <= 7:
                                    trb2((k - 3) // 2, 2)
                                if 0 <= k - 18 <= 31:
                                    main(k - 18)

                        @blk.vector
                        def _(ve):
                            vcc = [0]

                            def vinc(ins):
                                ins.then_inc(p2v, 1)
                                vcc[0] += 1
                                return vcc[0]

                            def redf(k):
                                if k >= 2:
                                    ve.wait_ge(p2t, TL[("bm4", k - 2)])
                                for q in range(4):
                                    gq = 4 * k + q
                                    ve.wait_ge(sm_mask[gq % 2],
                                               16 * (gq // 2 + 1))
                                    vinc(ve.tensor_reduce(
                                        red[k % 2][:, q * 32:(q + 1) * 32],
                                        mk[gq % 2][:, :].rearrange(
                                            "p (s e) -> p s e", e=32),
                                        AX.X, ALU.add))

                            def thresh(k):
                                ve.wait_ge(p2t, TL[("bm4", k)])
                                if k >= 2:
                                    ve.wait_ge(p2t, TL[("tr4", k - 2)])
                                vinc(ve.tensor_scalar(bm01t[k % 2][:, :],
                                                      p_bm[0:4, 0:128],
                                                      0.0, None, ALU.is_gt))

                            def feev(k):
                                ve.wait_ge(p2t, TL[("tr4", k)])
                                ix = PTRIX[("tr4", k)]
                                vinc(ve.tensor_copy(bmfeT[:, 4 * k:4 * k + 4],
                                                    p_tr[ix % 2][:, 0:4]))

                            def ln_a(j):
                                ve.wait_ge(sm_x[j % 2], 16 * (j // 2 + 1))
                                vinc(ve.reduce_sum(st_sum[:, j:j + 1],
                                                   xs[j % 2][:, :], AX.X))

                            def ln_b(j):
                                sl = slice(j, j + 1)
                                ve.wait_ge(p1s, s_sq(j))
                                ve.wait_ge(p2v, VL[("suma", j)])
                                vinc(ve.tensor_scalar(st_mean[:, sl], st_sum[:, sl],
                                                      ninvd_col[:, 0:1], None,
                                                      ALU.mult))
                                vinc(ve.tensor_scalar(st_msq[:, sl], st_ssq[:, sl],
                                                      invd_col[:, 0:1], None,
                                                      ALU.mult))
                                ve.wait_ge(p2v, vcc[0] - 1)
                                vinc(ve.tensor_tensor(st_m2[:, sl], st_mean[:, sl],
                                                      st_mean[:, sl], ALU.mult))
                                ve.wait_ge(p2v, vcc[0])
                                vinc(ve.tensor_tensor(st_var[:, sl], st_msq[:, sl],
                                                      st_m2[:, sl], ALU.subtract))
                                ve.wait_ge(p2v, vcc[0])
                                vinc(ve.tensor_scalar(st_sqv[:, sl],
                                                      st_var[:, sl],
                                                      eps_col[:, 0:1], None,
                                                      ALU.add))
                                ve.wait_ge(p1s, s_sqrt(j))
                                vinc(ve.reciprocal(st_rst[:, sl], st_var[:, sl]))
                                ve.wait_ge(p2v, vcc[0])
                                vinc(ve.tensor_tensor(st_nmr[:, sl], st_mean[:, sl],
                                                      st_rst[:, sl], ALU.mult))

                            def trev2(j, b0):
                                for b in (b0, b0 + 1):
                                    ve.wait_ge(p2t, TL[("trb", j, b)])
                                    ix = PTRIX[("trb", j, b)]
                                    dst = xnT[:, :].rearrange(
                                        "p (kc t) -> p kc t", t=T)[
                                        :, b * 8:(b + 1) * 8,
                                        j * 128:(j + 1) * 128]
                                    srcp = p_tr[ix % 2][:, :].rearrange(
                                        "p (c t) -> p c t", t=128)
                                    vinc(ve.tensor_copy(dst, srcp))

                            def mult(nt):
                                par = nt % 2
                                ve.wait_ge(sm_wt[nt % 2], 16 * (nt // 2 + 1))
                                ve.wait_ge(p2t, TL[("exp", nt // 4)])
                                if nt >= 2:
                                    ve.wait_ge(p2t, TL[("main", nt - 2)])
                                ap1 = bass.AP(p_me4.ap().tensor, (nt % 4) * 4,
                                              [[512, 128], [16, 32], [1, 4], [0, 32]])
                                vinc(ve.tensor_tensor(
                                    wtm[par][:, :].rearrange(
                                        "p (a b c) -> p a b c", a=32, b=4),
                                    wts[par][:, :].rearrange(
                                        "p (a b c) -> p a b c", a=32, b=4),
                                    ap1, ALU.mult))

                            def evac(nt):
                                par = nt % 2
                                ve.wait_ge(p2t, TL[("main", nt)])
                                if nt >= 2:
                                    ve.wait_ge(sm_sT_st[nt % 2], 16 * (nt // 2))
                                for th in range(2):
                                    vinc(ve.tensor_copy(
                                        sTs[par][:, th * 512:th * 512 + 512],
                                        p_sT[par][th][:, :]))
                            vinc(ve.memset(eps_col[:, :], EPS))
                            vinc(ve.memset(invd_col[:, :], 1.0 / D))
                            vinc(ve.memset(ninvd_col[:, :], -1.0 / D))
                            for k in range(NK):
                                if k <= 31:
                                    redf(k)
                                if 0 <= k - 1 <= 31:
                                    thresh(k - 1)
                                if 0 <= k - 2 <= 31:
                                    feev(k - 2)
                                if k >= 2 and (k - 2) % 2 == 0 and (k - 2) // 2 <= 7:
                                    trev2((k - 2) // 2, 0)
                                if k >= 3 and (k - 3) % 2 == 0 and (k - 3) // 2 <= 7:
                                    trev2((k - 3) // 2, 2)
                                if k <= 15:
                                    if k % 2 == 0:
                                        ln_a(k // 2)
                                    else:
                                        ln_b((k - 1) // 2)
                                if 0 <= k - 17 <= 31:
                                    mult(k - 17)
                                if 0 <= k - 19 <= 31:
                                    evac(k - 19)
                            assert vcc[0] == vc, (vcc[0], vc)

            with ExitStack() as s3:
                sb3l = lambda n, s, d: s3.enter_context(nc.sbuf_tensor(n, s, d))
                ps3 = lambda n, s, d: s3.enter_context(nc.psum_tensor(n, s, d))
                wgb = [sb3l(f"wgb{i}", [128, D], BF16) for i in range(2)]
                wub = [sb3l(f"wub{i}", [128, D], BF16) for i in range(2)]
                sil = [sb3l(f"sil{i}", [128, T], F32) for i in range(2)]
                uss = [sb3l(f"uss{i}", [128, T], F32) for i in range(2)]
                hTt = [sb3l(f"hTt{i}", [128, T], BF16) for i in range(2)]
                p_g = [[ps3(f"p_g{i}_{h}", [128, 512], F32) for h in range(2)]
                       for i in range(2)]
                p_u = [[ps3(f"p_u{i}_{h}", [128, 512], F32) for h in range(2)]
                       for i in range(2)]

                with nc.Block("ffnA") as blk:
                    @blk.gpsimd
                    def _(gp):
                        for it in range(NIT):
                            if it >= 2:
                                gp.wait_ge(p3t, it - 1)
                            gp.dma_start(
                                wgb[it % 2][:, :].rearrange("p (b i) -> p b i", i=128),
                                d_hwg[it]).then_inc(sm_wg[it % 2], 16)
                            gp.dma_start(
                                wub[it % 2][:, :].rearrange("p (b i) -> p b i", i=128),
                                d_hwu[it]).then_inc(sm_wu[it % 2], 16)
                        for gb in range(2):
                            gp.dma_start(
                                wdb[:, gb * 2048:gb * 2048 + 2048].rearrange(
                                    "p (i hh) -> p i hh", i=4),
                                d_hwd[gb * 4:gb * 4 + 4, :, 0, :, :].rearrange(
                                    "i p a b -> p i (a b)")
                                ).then_inc(sm_wd[gb], 16)

                    @blk.tensor
                    def _(te):
                        for it in range(NIT):
                            par = it % 2
                            te.wait_ge(sm_wg[it % 2], 16 * (it // 2 + 1))
                            te.wait_ge(sm_wu[it % 2], 16 * (it // 2 + 1))
                            if it >= 2:
                                te.wait_ge(p3s, 2 * (it - 2) + 2)
                                te.wait_ge(p3v, 3 * (it - 2) + 2)
                            for w_b, p_x in [(wgb[par], p_g[par]), (wub[par], p_u[par])]:
                                for blk_h in range(32):
                                    A = w_b[:, blk_h * 128:(blk_h + 1) * 128]
                                    for th in range(2):
                                        ins = te.matmul(
                                            p_x[th][:, :], A,
                                            xnT[:, blk_h * T + th * 512:
                                                blk_h * T + th * 512 + 512],
                                            start=(blk_h == 0), stop=(blk_h == 31))
                            ins.then_inc(p3t, 1)

                    @blk.scalar
                    def _(sc):
                        for it in range(NIT):
                            par = it % 2
                            if it >= 1:
                                sc.wait_ge(p3v, 3 * (it - 1) + 3)
                                sc.dma_start(d_hTd[it - 1], hTt[(it - 1) % 2][:, :]
                                             ).then_inc(sm_hT_st[(it - 1) % 2], 16)
                            sc.wait_ge(p3t, it + 1)
                            if it >= 2:
                                sc.wait_ge(p3v, 3 * (it - 2) + 3)
                            for th in range(2):
                                sc.activation(sil[par][:, th * 512:th * 512 + 512],
                                              p_g[par][th][:, :], AF.Silu,
                                              scale=sg_cols[:, it:it + 1]
                                              ).then_inc(p3s, 1)
                        sc.wait_ge(p3v, 3 * (NIT - 1) + 3)
                        sc.dma_start(d_hTd[NIT - 1], hTt[(NIT - 1) % 2][:, :]
                                     ).then_inc(sm_hT_st[(NIT - 1) % 2], 16)

                    @blk.vector
                    def _(ve):
                        for it in range(NIT):
                            par = it % 2
                            ve.wait_ge(p3t, it + 1)
                            if it >= 2:
                                ve.wait_ge(p3v, 3 * it - 3)
                            for th in range(2):
                                ve.tensor_scalar(uss[par][:, th * 512:th * 512 + 512],
                                                 p_u[par][th][:, :],
                                                 su_cols[:, it:it + 1], None,
                                                 ALU.mult).then_inc(p3v, 1)
                            ve.wait_ge(p3s, 2 * it + 2)
                            if it >= 2:
                                ve.wait_ge(sm_hT_st[it % 2], 16 * (it // 2))
                            ve.wait_ge(p3v, 3 * it + 2)
                            ve.tensor_tensor(hTt[par][:, :], sil[par][:, :],
                                             uss[par][:, :], ALU.mult).then_inc(p3v, 1)

        with ExitStack() as s4:
            sb4 = lambda n, s, d: s4.enter_context(nc.sbuf_tensor(n, s, d))
            ps4 = lambda n, s, d: s4.enter_context(nc.psum_tensor(n, s, d))
            hTr = sb4("hTr", [128, 48 * 1024], BF16)
            hTl = [sb4(f"hTl{i}", [128, 2048], BF16) for i in range(4)]
            gTs = [sb4(f"gTs{i}", [128, T], F32) for i in range(4)]
            sTl = [sb4(f"sTl{i}", [128, T], F32) for i in range(4)]
            xTl = [sb4(f"xTl{i}", [128, T], F32) for i in range(4)]
            outs = [sb4(f"outs{i}", [128, T], F32) for i in range(4)]
            tmpm = sb4("tmpm", [128, T], F32)
            p_acc = [[ps4(f"p_acc{h}_{t}", [128, 512], F32) for t in range(2)]
                     for h in range(4)]

            NB = 22
            NPAIR = 19
            SEQ = P4SEQ
            assert sorted(SEQ) == list(range(NIT))
            POS = {it: j for j, it in enumerate(SEQ)}
            ht_use = {}
            _cnt4 = [0, 0, 0, 0]
            for _g in range(8):
                for _p in range(NPAIR):
                    _cnt4[_p % 4] += 1
                    ht_use[(_g, _p)] = _cnt4[_p % 4]
            wd_use = {}
            _cnt4w = [0, 0, 0, 0]
            for _g in range(8):
                for _b in range(NB):
                    _gb = _g * NB + _b
                    _cnt4w[_gb % 4] += 1
                    wd_use[(_g, _b)] = _cnt4w[_gb % 4]

            with nc.Block("down") as blk:
                @blk.sync
                def _(sy):
                    for r in range(6):
                        sy.wait_ge(sm_hT_st[0], 16 * (4 * r + 4))
                        sy.wait_ge(sm_hT_st[1], 16 * (4 * r + 4))
                        sy.dma_start(
                            hTr[:, r * 8192:(r + 1) * 8192].rearrange(
                                "q (i t) -> q i t", i=8),
                            d_hTd[8 * r:8 * r + 8, :, :].rearrange(
                                "i q t -> q i t")).then_inc(sm_hr[r], 16)
                    sy.wait_ge(sm_hT_st[0], 16 * 43)
                    sy.wait_ge(sm_hT_st[1], 16 * 43)
                    sy.wait_ge(sm_sT_st[0], 16 * 16)
                    sy.wait_ge(sm_sT_st[1], 16 * 16)
                    for g in range(8):
                        for p in range(NPAIR):
                            if p >= 4:
                                sy.wait_ge(p4t,
                                           g * NIT + POS[48 + 2 * (p - 4) + 1] + 1)
                            elif g >= 1:
                                q = {0: 16, 1: 17, 2: 18, 3: 15}[p]
                                sy.wait_ge(p4t,
                                           (g - 1) * NIT + POS[48 + 2 * q + 1] + 1)
                            sy.dma_start(
                                hTl[p % 4][:, :].rearrange(
                                    "q (i t) -> q i t", i=2),
                                d_hTd[48 + 2 * p:48 + 2 * p + 2, :, :].rearrange(
                                    "i q t -> q i t")).then_inc(sm_hT_ld[p % 4], 16)
                            if p == 2:
                                for ht in range(4):
                                    hg = g * 4 + ht
                                    if g >= 1:
                                        sy.wait_ge(p4v, 8 * (g - 1) + 2 * (ht + 1))
                                    sy.dma_start(xTl[ht][:, :],
                                                 d_xT[hg * 128:(hg + 1) * 128, :]
                                                 ).then_inc(sm_xT[ht], 16)
                    for ht in range(4):
                        sy.wait_ge(sm_out[ht], 16 * 8)

                @blk.gpsimd
                def _(gp):
                    for g in range(8):
                        for b in range(NB):
                            it0 = b * 4
                            nit = min(4, NIT - it0)
                            gb = g * NB + b
                            if gb < 2:
                                continue
                            if gb >= 4:
                                pgb = gb - 4
                                pg, pb = pgb // NB, pgb % NB
                                gp.wait_ge(p4t,
                                           pg * NIT + min(4 * (pb + 1), NIT))
                            sl = (gb % 4) * 2048
                            dst = wdb[:, sl:sl + nit * 512]
                            gp.dma_start(
                                dst.rearrange("p (i hh) -> p i hh", i=nit),
                                d_hwd[it0:it0 + nit, :, g, :, :].rearrange(
                                    "i p a b -> p i (a b)")
                                ).then_inc(sm_wd[gb % 4], 16)
                        for ht in range(4):
                            hg = g * 4 + ht
                            if g >= 1:
                                gp.wait_ge(p4v, 8 * (g - 1) + 2 * (ht + 1))
                            gp.wait_ge(sm_sT_st[hg % 2], 16 * (hg // 2 + 1))
                            gp.dma_start(sTl[ht][:, :],
                                         d_sTd[hg * 128:(hg + 1) * 128, :]
                                         ).then_inc(sm_sT_ld[ht], 16)

                @blk.tensor
                def _(te):
                    for g in range(8):
                        for j in range(NIT):
                            it = SEQ[j]
                            b = j // 4
                            gb = g * NB + b
                            te.wait_ge(sm_wd[gb % 4], 16 * wd_use[(g, b)])
                            if it < 48:
                                if g == 0:
                                    te.wait_ge(sm_hr[it // 8], 16)
                                slot, sbase = hTr, it * 1024
                            else:
                                pr = (it - 48) // 2
                                te.wait_ge(sm_hT_ld[pr % 4],
                                           16 * ht_use[(g, pr)])
                                slot, sbase = hTl[pr % 4], (it % 2) * 1024
                            base = (gb % 4) * 2048 + (j % 4) * 512
                            for ht in range(4):
                                A = wdb[:, base + ht * 128:base + ht * 128 + 128]
                                for th in range(2):
                                    if j == 0 and g >= 1:
                                        te.wait_ge(p4s,
                                                   8 * (g - 1) + 2 * ht + th + 1)
                                    B = slot[:, sbase + th * 512:
                                             sbase + th * 512 + 512]
                                    ins = te.matmul(p_acc[ht][th][:, :], A, B,
                                                    start=(j == 0), stop=(j == NIT - 1))
                            ins.then_inc(p4t, 1)

                @blk.scalar
                def _(sc):
                    for g in range(8):
                        sc.wait_ge(p4t, g * NIT + NIT)
                        for ht in range(4):
                            if g >= 1:
                                sc.wait_ge(p4v, 8 * (g - 1) + 2 * (ht + 1))
                            for th in range(2):
                                sc.activation(gTs[ht][:, th * 512:th * 512 + 512],
                                              p_acc[ht][th][:, :], AF.Silu,
                                              scale=sd_cols[:, g * 4 + ht:g * 4 + ht + 1]
                                              ).then_inc(p4s, 1)
                        for ht in range(4):
                            hg = g * 4 + ht
                            sc.wait_ge(p4v, 8 * g + 2 * (ht + 1))
                            sc.dma_start(d_out[hg * 128:(hg + 1) * 128, :],
                                         outs[ht][:, :]).then_inc(sm_out[ht], 16)

                @blk.vector
                def _(ve):
                    for g in range(8):
                        for ht in range(4):
                            k = 4 * g + ht
                            ve.wait_ge(p4s, 8 * g + 2 * (ht + 1))
                            ve.wait_ge(sm_xT[ht], 16 * (g + 1))
                            ve.wait_ge(sm_sT_ld[ht], 16 * (g + 1))
                            if g >= 1:
                                ve.wait_ge(sm_out[ht], 16 * g)
                            if k >= 1:
                                ve.wait_ge(p4v, 2 * k)
                            ve.tensor_tensor(tmpm[:, :], gTs[ht][:, :], sTl[ht][:, :],
                                             ALU.mult).then_inc(p4v, 1)
                            ve.wait_ge(p4v, 2 * k + 1)
                            ve.tensor_tensor(outs[ht][:, :], tmpm[:, :], xTl[ht][:, :],
                                             ALU.add).then_inc(p4v, 1)

    nc.finalize()
    return nc


_NC_CACHE = None


def kernel(L0_attn_data, L0_attn_weight, L0_attn_mask, L0_ffn_w_gate, L0_ffn_s_gate,
           L0_ffn_w_up, L0_ffn_s_up, L0_ffn_w_down, L0_ffn_s_down):
    global _NC_CACHE
    asc = np.ascontiguousarray

    x2 = np.asarray(L0_attn_data, dtype=np.float32).reshape(NCORES * T, D)
    w = np.asarray(L0_attn_weight, dtype=np.float32)
    mask = np.asarray(L0_attn_mask, dtype=np.int32)
    wg = np.asarray(L0_ffn_w_gate, dtype=np.int32)
    wu = np.asarray(L0_ffn_w_up, dtype=np.int32)
    wd = np.asarray(L0_ffn_w_down, dtype=np.int32)
    sg = np.asarray(L0_ffn_s_gate, dtype=np.float32)
    su = np.asarray(L0_ffn_s_up, dtype=np.float32)
    sd = np.asarray(L0_ffn_s_down, dtype=np.float32)

    hwt = asc(w.reshape(32, 128, 32, 128).transpose(0, 3, 2, 1))
    hwg = asc(wg.reshape(NIT, 128, 32, 128).transpose(0, 3, 2, 1))
    hwu = asc(wu.reshape(NIT, 128, 32, 128).transpose(0, 3, 2, 1))
    hwd = asc(wd.reshape(8, 4, 128, NIT, 128).transpose(3, 4, 0, 1, 2)[P4SEQ])
    sgT = asc(sg.reshape(NIT, 128).T)
    suT = asc(su.reshape(NIT, 128).T)
    sdT = asc(sd.reshape(32, 128).T)

    c = np.arange(D)
    p = np.arange(128)[:, None]
    e32f = (c // 32 == p).astype(ml_dtypes.bfloat16)
    gmat4 = (np.arange(4) == p // 32).astype(np.float32)
    identb = np.eye(128, dtype=ml_dtypes.bfloat16)

    shared = dict(hwt=hwt, mask=mask, hwg=hwg, hwu=hwu, hwd=hwd, sgT=sgT, suT=suT,
                  sdT=sdT, e32f=e32f, gmat4=gmat4, identb=identb)
    in_maps = []
    for ci in range(NCORES):
        xc = asc(x2[ci * T:(ci + 1) * T])
        in_maps.append(dict(x=xc, xT=asc(xc.T), **shared))

    if _NC_CACHE is None:
        _NC_CACHE = build_nc()
    res = run_bass_kernel_spmd(_NC_CACHE, in_maps, core_ids=list(range(NCORES)))
    out = np.concatenate([np.asarray(r["outT"]).T for r in res.results], axis=0)
    return asc(out.reshape(4, 2048, D).astype(np.float32))
